# version 02 (final)
# baseline (speedup 1.0000x reference)
import os
import contextlib
import numpy as np

import concourse.bass as bass
import concourse.bacc as bacc
import concourse.mybir as mybir
from concourse.bass_utils import run_bass_kernel_spmd

F32 = mybir.dt.float32
BF16 = mybir.dt.bfloat16
I32 = mybir.dt.int32
AF = mybir.ActivationFunctionType
ALU = mybir.AluOpType
AX = mybir.AxisListType.X
ET = mybir.EngineType

B, S, D = 16, 2048, 1024
HID, NE = 4096, 8
NH, DH = 16, 64
NCORES = 8
NB = B // NCORES
TLOC = NB * S
LN_EPS = 1e-5
GN_EPS = 1e-6

FT = 256
NFT = TLOC // FT
P1T = 512
NP1 = TLOC // P1T

_STAGE = int(os.environ.get("MK_DEV_STAGE", "99"))
_P1CUT = int(os.environ.get("MK_P1CUT", "99"))
_P1B = int(os.environ.get("MK_P1B", "0"))
_P1N = int(os.environ.get("MK_P1N", "0"))

MAN0 = 40960
OFF_XIN = MAN0
OFF_AB = OFF_XIN + 16384
OFF_BB = OFF_AB + 1024
OFF_WB = OFF_BB + 1024
OFF_TMP = OFF_WB + 1024
OFF_ROW = OFF_TMP + 1024
F0 = OFF_ROW + 6144
OFF_W1 = F0
OFF_W2 = OFF_W1 + 65536
OFF_HT = OFF_W2 + 65536
OFF_YT = OFF_HT + 16384
FFN_END = OFF_YT + 8192
assert FFN_END <= 229376, FFN_END
OFF_P1X = F0
OFF_LG = F0 + 32768
OFF_SQ = F0 + 34816
OFF_A8 = F0 + 38912
OFF_RM = F0 + 40960
OFF_B2T = F0
OFF_W8S = F0 + 4096
OFF_Y2 = F0
OFF_AO = F0 + 32768
OFF_Q2 = F0 + 65536
OFF_K2 = OFF_Q2 + 4096
OFF_V2 = OFF_K2 + 4096
OFF_VT = OFF_V2 + 4096
OFF_PSB = OFF_VT + 4096
OFF_PBF = OFF_PSB + 8192
OFF_PT = OFF_PBF + 4096
OFF_WQ = OFF_PT + 4096
OFF_WO = OFF_WQ + 12288
OFF_A2R = OFF_WO + 16384
OFF_B2R = OFF_A2R + 8192
OFF_DSB = OFF_B2R + 8192
OFF_EM = OFF_DSB + 2048
OFF_GM = OFF_EM + 4096
OFF_AST = OFF_GM + 1024
ATT_END = OFF_AST + 4096
assert ATT_END <= 229376, ATT_END
OFF_TIN = F0
OFF_TOUT = F0 + 8192


class MK:
    def __init__(self):
        self.nc = bacc.Bacc("TRN2", target_bir_lowering=False,
                            detect_race_conditions=False)
        self.ctx = contextlib.ExitStack()

    def sem(self, name):
        return self.ctx.enter_context(self.nc.semaphore(name))

    def sb(self, name, shape, dtype):
        return self.ctx.enter_context(self.nc.sbuf_tensor(name, shape, dtype))

    def sb_at(self, name, shape, dtype, off):
        return self.nc.alloc_sbuf_tensor_at(name, shape, dtype, offset=off)

    def ps(self, name, shape, dtype=F32):
        return self.ctx.enter_context(self.nc.psum_tensor(name, shape, dtype))


def build(stage=99):
    mk = MK()
    nc = mk.nc

    x_in = nc.dram_tensor("x", [TLOC, D], F32, kind="ExternalInput")
    wqkv = nc.dram_tensor("wqkv", [D, 3 * D], F32, kind="ExternalInput")
    bqkv = nc.dram_tensor("bqkv", [3 * D], F32, kind="ExternalInput")
    wo = nc.dram_tensor("wo", [D, D], F32, kind="ExternalInput")
    bo = nc.dram_tensor("bo", [D], F32, kind="ExternalInput")
    gng = nc.dram_tensor("gng", [D], F32, kind="ExternalInput")
    gnb = nc.dram_tensor("gnb", [D], F32, kind="ExternalInput")
    w1 = nc.dram_tensor("w1", [NE, D, HID], F32, kind="ExternalInput")
    b1 = nc.dram_tensor("b1", [NE, HID], F32, kind="ExternalInput")
    w2 = nc.dram_tensor("w2", [NE, HID, D], F32, kind="ExternalInput")
    b2 = nc.dram_tensor("b2", [NE, D], F32, kind="ExternalInput")
    wr = nc.dram_tensor("wr", [D, NE], F32, kind="ExternalInput")
    br = nc.dram_tensor("br", [NE], F32, kind="ExternalInput")
    out = nc.dram_tensor("out", [TLOC, D], F32, kind="ExternalOutput")

    xT_d = nc.dram_tensor("xT_d", [D, TLOC], F32, kind="Internal")
    arow_d = nc.dram_tensor("arow_d", [TLOC], F32, kind="Internal")
    brow_d = nc.dram_tensor("brow_d", [TLOC], F32, kind="Internal")
    wrow_d = nc.dram_tensor("wrow_d", [NE, TLOC], F32, kind="Internal")
    w8t_d = nc.dram_tensor("w8t_d", [NE, TLOC], F32, kind="Internal")
    yt_d = nc.dram_tensor("yt_d", [D, TLOC], BF16, kind="Internal")
    dbg_d = None
    if stage < 99:
        dbg_d = nc.dram_tensor("dbg", [D, TLOC], F32, kind="ExternalOutput")
        dbg2_d = nc.dram_tensor("dbg2", [D, TLOC], BF16, kind="ExternalOutput")

    import ml_dtypes
    identf = nc.inline_tensor(np.eye(128, dtype=np.float32), "identf")
    identb_d = nc.inline_tensor(np.eye(128, dtype=ml_dtypes.bfloat16), "identb")
    onesc_d = nc.inline_tensor(np.ones((128, 1), np.float32), "onesc")
    onesr_d = nc.inline_tensor(np.ones((1, 128), np.float32), "onesr")
    ones512_d = nc.inline_tensor(np.ones((1, 512), np.float32), "ones512")
    ones512b_d = nc.inline_tensor(np.ones((1, 512), ml_dtypes.bfloat16), "ones512b")
    g_np = np.zeros((128, 8 * 32), np.float32)
    e_np = np.zeros((32, 8 * 128), np.float32)
    for kc in range(8):
        for p in range(128):
            gg = 4 * kc + p // 32
            g_np[p, kc * 32 + gg] = 1.0
            e_np[gg, kc * 128 + p] = 1.0
    gmat_d = nc.inline_tensor(g_np, "gmat")
    emat_d = nc.inline_tensor(e_np, "emat")

    identS = mk.sb("identS", [128, 128], F32)
    identBS = mk.sb("identBS", [128, 128], BF16)
    onescS = mk.sb("onescS", [128, 1], F32)
    onesrS = mk.sb("onesrS", [1, 128], F32)
    ones512S = mk.sb("ones512S", [1, 512], F32)
    ones512B = mk.sb("ones512B", [1, 512], BF16)
    borowB = mk.sb("borowB", [1, 1024], BF16)
    borow = mk.sb("borow", [1, 1024], F32)
    bqr = [mk.sb(f"bqr{i}", [1, 384], BF16) for i in range(2)]
    wrS = mk.sb("wrS", [128, 64], F32)
    brC = mk.sb("brC", [NE, 1], F32)
    b1S = mk.sb("b1S", [128, 256], F32)
    b1cur = mk.sb("b1cur", [128, 32], F32)
    bqkvS = mk.sb("bqkvS", [128, 24], F32)
    bqkvSq = mk.sb("bqkvSq", [128, 8], F32)
    boS = mk.sb("boS", [128, 8], F32)
    gamS = mk.sb("gamS", [128, 8], F32)
    betS = mk.sb("betS", [128, 8], F32)
    seR = mk.sb("seR", [1, 8], F32)
    seC = mk.sb("seC", [8, 1], F32)
    stg128 = mk.sb("stg128", [128, 128], F32)
    wtok = mk.sb("wtok", [128, 8], F32)
    wtk2 = mk.sb("wtk2", [128, 8], F32)
    wtkM = mk.sb("wtkM", [128, 8], I32)
    wtkF = mk.sb("wtkF", [128, 8], F32)
    colM = mk.sb("colM", [128, 1], I32)
    negbig = mk.sb("negbig", [128, 8], F32)
    zero8 = mk.sb("zero8", [128, 8], F32)
    col1 = mk.sb("col1", [128, 1], F32)
    col2 = mk.sb("col2", [128, 1], F32)
    col3 = mk.sb("col3", [128, 1], F32)
    col4 = mk.sb("col4", [128, 1], F32)
    w8rowstg = [mk.sb(f"w8rowstg{i}", [NE, 128], F32) for i in range(2)]
    gnS = mk.sb("gnS", [128, 8], F32)
    gnQ = mk.sb("gnQ", [128, 8], F32)
    gnpart = mk.sb("gnpart", [128, 64], F32)
    gnqart = mk.sb("gnqart", [128, 64], F32)
    g32a = mk.sb("g32a", [32, 1], F32)
    g32b = mk.sb("g32b", [32, 1], F32)
    g32c = mk.sb("g32c", [32, 1], F32)
    wcol1 = mk.sb("wcol1", [32, 1], F32)
    gcolS = mk.sb("gcolS", [32, 1], F32)
    gcolQ = mk.sb("gcolQ", [32, 1], F32)
    aknS = mk.sb("aknS", [128, 8], F32)
    bknS = mk.sb("bknS", [128, 8], F32)
    bq_cur = mk.sb("bq_cur", [128, 1], F32)
    bk_cur = mk.sb("bk_cur", [128, 1], F32)
    bv_cur = mk.sb("bv_cur", [128, 1], F32)
    mxcol = mk.sb("mxcol", [128, 1], F32)
    nmxcol = mk.sb("nmxcol", [128, 1], F32)
    dncol = mk.sb("dncol", [128, 1], F32)
    rccol = mk.sb("rccol", [128, 1], F32)

    xin = [mk.sb_at(f"xin{i}", [128, 2048], F32, OFF_XIN + i * 8192) for i in range(2)]
    ab_sb = mk.sb_at("ab_sb", [128, 256], F32, OFF_AB)
    bb_sb = mk.sb_at("bb_sb", [128, 256], F32, OFF_BB)
    wb_sb = mk.sb_at("wb_sb", [128, 256], F32, OFF_WB)
    tmpbuf = mk.sb_at("tmpbuf", [128, 256], F32, OFF_TMP)
    rowstg = [mk.sb_at(f"rowstg{i}", [1, 768], F32, OFF_ROW + i * 3072) for i in range(2)]
    w1S = mk.sb_at("w1S", [128, 32768], BF16, OFF_W1)
    w2S = mk.sb_at("w2S", [128, 32768], BF16, OFF_W2)
    hT = mk.sb_at("hT", [128, 8192], BF16, OFF_HT)
    yT = [mk.sb_at(f"yTp{i}", [128, 2048], BF16, OFF_YT + i * 4096) for i in range(2)]
    p1x = [mk.sb_at(f"p1x{i}", [128, 4096], F32, OFF_P1X + i * 16384) for i in range(2)]
    lgbuf = mk.sb_at("lgbuf", [8, 512], F32, OFF_LG)
    sqbuf = [mk.sb_at(f"sqbuf{i}", [128, 512], F32, OFF_SQ + i * 2048) for i in range(2)]
    sebS = mk.sb_at("sebS", [8, 512], F32, OFF_A8)
    rawS = mk.sb_at("rawS", [8, 512], F32, OFF_HT)
    tmp2 = mk.sb_at("tmp2", [128, 256], F32, FFN_END)
    p1ab = mk.sb_at("p1ab", [128, 512], F32, OFF_RM + 18432)
    p1bb = mk.sb_at("p1bb", [128, 512], F32, OFF_RM + 20480)
    ystg = [mk.sb_at(f"ystg{i}", [128, 4096], BF16, OFF_W2 + i * 8192) for i in range(2)]
    rowM = mk.sb_at("rowM", [1, 512], F32, OFF_RM)
    rowQ = mk.sb_at("rowQ", [1, 512], F32, OFF_RM + 2048)
    rowT = mk.sb_at("rowT", [1, 512], F32, OFF_RM + 4096)
    rowA = mk.sb_at("rowA", [1, 512], F32, OFF_RM + 6144)
    rowB = mk.sb_at("rowB", [1, 512], F32, OFF_RM + 8192)
    wrowstg8 = [mk.sb_at(f"wrowstg8_{i}", [1, 1024], F32, OFF_RM + 10240 + i * 4096) for i in range(2)]
    b2TS = mk.sb_at("b2TS", [8, 1024], F32, OFF_B2T)
    w8stg = [mk.sb_at(f"w8stg{i}", [8, 256], F32, OFF_W8S + i * 1024) for i in range(2)]
    y2S = mk.sb_at("y2S", [128, 16384], BF16, OFF_Y2)
    aoS = mk.sb_at("aoS", [128, 16384], BF16, OFF_AO)
    q2S = mk.sb_at("q2S", [128, 2048], BF16, OFF_Q2)
    k2S = mk.sb_at("k2S", [128, 2048], BF16, OFF_K2)
    v2S = mk.sb_at("v2S", [128, 2048], BF16, OFF_V2)
    vtS = mk.sb_at("vtS", [128, 2048], BF16, OFF_VT)
    psbS = mk.sb_at("psbS", [128, 2048], F32, OFF_PSB)
    pbfS = mk.sb_at("pbfS", [128, 2048], BF16, OFF_PBF)
    ptS = mk.sb_at("ptS", [128, 2048], BF16, OFF_PT)
    wqS = [mk.sb_at(f"wqS{i}", [128, 3072], BF16, OFF_WQ + i * 6144) for i in range(2)]
    woS = mk.sb_at("woS", [128, 8192], BF16, OFF_WO)
    a2R = mk.sb_at("a2R", [1, 2048], F32, OFF_A2R)
    b2R = mk.sb_at("b2R", [1, 2048], F32, OFF_B2R)
    gmatS = mk.sb("gmatS", [128, 256], F32)
    ematS = mk.sb("ematS", [32, 1024], F32)
    dsb = [mk.sb(f"dsb{i}", [128, 256], F32) for i in range(2)]
    aostg = mk.sb_at("aostg", [128, 2048], BF16, OFF_AST)
    tin = [mk.sb_at(f"tin{i}", [128, 1024], F32, OFF_TIN + i * 4096) for i in range(2)]
    tout = [mk.sb_at(f"tout{i}", [128, 1024], F32, OFF_TOUT + i * 4096) for i in range(2)]

    PS_A = mk.ps("PS_A", [128, 2048], F32)
    PS_B = mk.ps("PS_B", [128, 1024], F32)
    PS_C = mk.ps("PS_C", [128, 512], F32)
    PS_D = mk.ps("PS_D", [128, 1024], BF16)

    s_ld0 = mk.sem("s_ld0")
    s_bb0 = mk.sem("s_bb0")
    s_c0 = mk.sem("s_c0")
    with nc.Block() as blk:
        @blk.sync
        def _(sp):
            sp.dma_start(identS[:, :], identf[:, :]).then_inc(s_ld0, 16)
            sp.dma_start(identBS[:, :], identb_d[:, :]).then_inc(s_ld0, 16)
            sp.dma_start(onescS[:, :], onesc_d[:, :]).then_inc(s_ld0, 16)
            sp.dma_start(onesrS[:, :], onesr_d[:, :]).then_inc(s_ld0, 16)
            sp.dma_start(ones512S[:, :], ones512_d[:, :]).then_inc(s_ld0, 16)
            sp.dma_start(ones512B[:, :], ones512b_d[:, :]).then_inc(s_ld0, 16)
            sp.dma_start(borow[:, :], bass.AP(bo, 0, [[1024, 1], [1, 1024]])).then_inc(s_ld0, 16)
            sp.dma_start(wrS[:, :].rearrange("p (kc e) -> p kc e", kc=8),
                         bass.AP(wr, 0, [[8, 128], [1024, 8], [1, 8]])).then_inc(s_ld0, 16)
            sp.dma_start(brC[:, :], bass.AP(br, 0, [[1, 8], [1, 1]])).then_inc(s_ld0, 16)
            sp.dma_start(gmatS[:, :], gmat_d[:, :]).then_inc(s_ld0, 16)
            sp.dma_start(ematS[:, :], emat_d[:, :]).then_inc(s_ld0, 16)

        @blk.vector
        def _(v):
            v.memset(negbig[:, :], -1e30)
            v.memset(zero8[:, :], 0.0)

        @blk.gpsimd
        def _(g):
            g.dma_start(borowB[:, :], bass.AP(bo, 0, [[1024, 1], [1, 1024]])).then_inc(s_bb0, 16)
            g.wait_ge(s_bb0, 16)

    with nc.Block() as blk:
        @blk.sync
        def _(sp):
            sp.dma_start(stg128[:, :], bass.AP(b1, 0, [[128, 128], [1, 128]])).then_inc(s_ld0, 16)

        @blk.tensor
        def _(t):
            t.wait_ge(s_ld0, 12 * 16)
            t.transpose(PS_C[:, 0:128], stg128[:, :], identS[:, :]).then_inc(s_c0, 1)

        @blk.vector
        def _(v):
            v.wait_ge(s_c0, 1)
            v.tensor_copy(b1S[:, 0:128], PS_C[:, 0:128]).then_inc(s_c0, 1)
    with nc.Block() as blk:
        @blk.sync
        def _(sp):
            sp.dma_start(stg128[:, :], bass.AP(b1, 128 * 128, [[128, 128], [1, 128]])).then_inc(s_ld0, 16)

        @blk.tensor
        def _(t):
            t.wait_ge(s_ld0, 13 * 16)
            t.wait_ge(s_c0, 2)
            t.transpose(PS_C[:, 0:128], stg128[:, :], identS[:, :]).then_inc(s_c0, 1)

        @blk.vector
        def _(v):
            v.wait_ge(s_c0, 3)
            v.tensor_copy(b1S[:, 128:256], PS_C[:, 0:128]).then_inc(s_c0, 1)
    with nc.Block() as blk:
        @blk.sync
        def _(sp):
            sp.dma_start(stg128[0:24, :], bass.AP(bqkv, 0, [[128, 24], [1, 128]])).then_inc(s_ld0, 16)

        @blk.tensor
        def _(t):
            t.wait_ge(s_ld0, 14 * 16)
            t.wait_ge(s_c0, 4)
            t.transpose(PS_C[:, 0:24], stg128[0:24, :], identS[0:24, 0:24]).then_inc(s_c0, 1)

        @blk.vector
        def _(v):
            v.wait_ge(s_c0, 5)
            v.tensor_copy(bqkvS[:, :], PS_C[:, 0:24]).then_inc(s_c0, 1)
            v.tensor_scalar(bqkvSq[:, :], bqkvS[:, 0:8], 0.125, None, ALU.mult).then_inc(s_c0, 1)
    with nc.Block() as blk:
        @blk.sync
        def _(sp):
            sp.dma_start(stg128[0:8, :], bass.AP(bo, 0, [[128, 8], [1, 128]])).then_inc(s_ld0, 16)
            sp.dma_start(stg128[8:16, :], bass.AP(gng, 0, [[128, 8], [1, 128]])).then_inc(s_ld0, 16)
            sp.dma_start(stg128[16:24, :], bass.AP(gnb, 0, [[128, 8], [1, 128]])).then_inc(s_ld0, 16)

        @blk.tensor
        def _(t):
            t.wait_ge(s_ld0, 17 * 16)
            t.wait_ge(s_c0, 7)
            t.transpose(PS_C[:, 0:24], stg128[0:24, :], identS[0:24, 0:24]).then_inc(s_c0, 1)
            for kc in range(8):
                ins = nc.tensor.matmul(PS_B[0:8, 0:1], wrS[:, kc * 8:(kc + 1) * 8],
                                       onescS[:, :], start=(kc == 0), stop=(kc == 7))
            ins.then_inc(s_c0, 1)

        @blk.vector
        def _(v):
            v.wait_ge(s_c0, 9)
            v.tensor_copy(boS[:, :], PS_C[:, 0:8])
            v.tensor_copy(gamS[:, :], PS_C[:, 8:16])
            v.tensor_copy(betS[:, :], PS_C[:, 16:24])
            v.tensor_copy(seC[:, :], PS_B[0:8, 0:1]).then_inc(s_c0, 1)
    with nc.Block() as blk:
        @blk.tensor
        def _(t):
            t.wait_ge(s_c0, 10)
            t.transpose(PS_C[0:1, 0:8], seC[:, :], identS[0:8, 0:8]).then_inc(s_c0, 1)

        @blk.scalar
        def _(sc):
            sc.wait_ge(s_c0, 11)
            sc.copy(seR[:, :], PS_C[0:1, 0:8]).then_inc(s_c0, 1)

    ph1_ld = mk.sem("ph1_ld")
    ph1_tp = mk.sem("ph1_tp")
    ph1_ev = mk.sem("ph1_ev")
    ph1_st = mk.sem("ph1_st")
    with nc.Block() as blk:
        @blk.sync
        def _(sp):
            sp.sem_inc(ph1_st, 2 * 128)
        @blk.vector
        def _(v):
            v.sem_inc(ph1_ev, 2)
        @blk.tensor
        def _(t):
            t.sem_inc(ph1_tp, 2)
    with nc.Block():
        with nc.Fori(0, 16, engines=[ET.Pool, ET.PE, ET.DVE, ET.SP]) as i:
            for par in range(2):
                tix = i * 2 + par
                nc.gpsimd.wait_ge(ph1_tp, tix + 1)
                nc.gpsimd.dma_start(tin[par][:, :],
                                    bass.AP(x_in, tix * 128 * D, [[D, 128], [1, D]])).then_inc(ph1_ld, 16)
            for par in range(2):
                tix = i * 2 + par
                psq = PS_B[:, :] if par == 0 else PS_A[:, 0:1024]
                nc.tensor.wait_ge(ph1_ld, (tix + 1) * 16)
                nc.tensor.wait_ge(ph1_ev, tix + 1)
                for j in range(8):
                    ins = nc.tensor.transpose(psq[:, j * 128:(j + 1) * 128],
                                              tin[par][:, j * 128:(j + 1) * 128],
                                              identS[:, :])
                ins.then_inc(ph1_tp, 1)
            for par in range(2):
                tix = i * 2 + par
                psq = PS_B[:, :] if par == 0 else PS_A[:, 0:1024]
                nc.vector.wait_ge(ph1_tp, tix + 3)
                nc.vector.wait_ge(ph1_st, 128 * tix + 128)
                nc.vector.tensor_copy(tout[par][:, :], psq).then_inc(ph1_ev, 1)
            for par in range(2):
                tix = i * 2 + par
                nc.sync.wait_ge(ph1_ev, tix + 3)
                for j in range(8):
                    nc.sync.dma_start(
                        bass.AP(xT_d, j * 128 * TLOC + tix * 128, [[TLOC, 128], [1, 128]]),
                        tout[par][:, j * 128:(j + 1) * 128]).then_inc(ph1_st, 16)
    with nc.Block() as blk:
        @blk.sync
        def _(sp):
            sp.wait_ge(ph1_st, 2 * 128 + 32 * 8 * 16)

    if stage < 99 and dbg_d is not None:
        dbg_sem = mk.sem("dbg_sem")
        with nc.Block() as blk:
            @blk.sync
            def _(sp):
                sp.dma_start(dbg_d[:, :], xT_d[:, :]).then_inc(dbg_sem, 16)
                sp.wait_ge(dbg_sem, 16)

    f1 = {n: mk.sem("f1_" + n) for n in
          ["ld", "xf", "ch", "sq", "qm", "rst", "tt", "w1", "ex", "w2",
           "wtr", "wc", "wst", "yst", "dv"]}
    f2 = {n: mk.sem("f2_" + n) for n in
          ["wld", "rld", "yld", "bm", "abe", "yd", "hr", "hg", "wr", "we",
           "st", "b2l", "b8r", "pb", "pbe", "pe2", "bst"]}

    with nc.Block() as blk:
        @blk.vector
        def _(v):
            v.sem_inc(f1["xf"], 2)
            v.sem_inc(f1["wst"], 2 * 144)
            v.sem_inc(f1["wc"], 2)
            v.sem_inc(f1["qm"], 2)
            v.sem_inc(f1["yst"], 32)
            v.sem_inc(f2["abe"], 1)
            v.sem_inc(f2["yd"], 2)
            v.sem_inc(f2["bm"], 2)
            v.sem_inc(f2["hg"], 2)
            v.sem_inc(f2["we"], 2)
            v.sem_inc(f2["pb"], 8)
            v.sem_inc(f2["pbe"], 2)

        @blk.gpsimd
        def _(g):
            for _i in range(2):
                g.dma_start(bass.AP(xT_d, 0, [[1, 1], [1, 1]]),
                            bass.AP(xT_d, 0, [[1, 1], [1, 1]])).then_inc(f2["st"], 16)
                g.dma_start(bass.AP(xT_d, 0, [[1, 1], [1, 1]]),
                            bass.AP(xT_d, 0, [[1, 1], [1, 1]])).then_inc(f2["bst"], 16)
            g.wait_ge(f2["st"], 32)
            g.wait_ge(f2["bst"], 32)

    def emit_ffn_pass1(invb):
        C = 9
        ch0 = invb * C * NP1

        def chp(t8, k):
            return ch0 + C * t8 + k

        with nc.Block():
            with nc.Fori(0, 4, engines=[ET.Pool, ET.PE, ET.DVE, ET.Activation, ET.SP]) as i4:
                for par in range(2):
                    t8 = i4 * 2 + par
                    g8 = invb * NP1 + t8
                    nc.gpsimd.wait_ge(f1["xf"], g8 + 1)
                    nc.gpsimd.dma_start(
                        p1x[par][:, :].rearrange("p (kc t) -> p kc t", kc=8),
                        bass.AP(xT_d, t8 * P1T, [[TLOC, 128], [128 * TLOC, 8], [1, P1T]])
                    ).then_inc(f1["ld"], 16)
                    for kc in range(8):
                        nc.vector.wait_ge(f1["ld"], 128 * invb + (t8 + 1) * 16)
                        nc.vector.wait_ge(f1["qm"], 8 * g8 + kc + 1)
                        if _P1CUT >= 1:
                            nc.vector.tensor_mul(sqbuf[kc % 2][:, :],
                                                 p1x[par][:, kc * 512:(kc + 1) * 512],
                                                 p1x[par][:, kc * 512:(kc + 1) * 512]).then_inc(f1["sq"], 1)
                        else:
                            nc.vector.engine_nop().then_inc(f1["sq"], 1)
                    nc.tensor.wait_ge(f1["ld"], 128 * invb + (t8 + 1) * 16)
                    nc.tensor.wait_ge(f1["ch"], chp(t8 - 1, 9))
                    nc.tensor.wait_ge(f1["wc"], 2 + 32 * invb + 4 * t8)
                    if _P1CUT >= 2:
                        for kc in range(8):
                            ins = nc.tensor.matmul(PS_C[0:1, 0:512], onescS[:, :],
                                                   p1x[par][:, kc * 512:(kc + 1) * 512],
                                                   start=(kc == 0), stop=(kc == 7))
                        for kc in range(8):
                            ins = nc.tensor.matmul(PS_B[0:8, 512:1024], wrS[:, kc * 8:(kc + 1) * 8],
                                                   p1x[par][:, kc * 512:(kc + 1) * 512],
                                                   start=(kc == 0), stop=(kc == 7))
                        for kc in range(8):
                            nc.tensor.wait_ge(f1["sq"], 64 * invb + 8 * t8 + kc + 1)
                            ins = nc.tensor.matmul(PS_B[0:1, 0:512], onescS[:, :],
                                                   sqbuf[kc % 2][:, :],
                                                   start=(kc == 0), stop=(kc == 7))
                            ins.then_inc(f1["qm"], 1)
                    else:
                        for kc in range(8):
                            nc.tensor.wait_ge(f1["sq"], 64 * invb + 8 * t8 + kc + 1)
                            nc.tensor.nop().then_inc(f1["qm"], 1)
                    if _P1N == 1 and _P1CUT >= 2:
                        ins.then_inc(f1["ch"], 1)
                    else:
                        nc.tensor.nop().then_inc(f1["ch"], 1)
                    nc.scalar.wait_ge(f1["ch"], chp(t8, 1))
                    nc.scalar.wait_ge(f1["qm"], 8 * g8 + 10)
                    if _P1B == 0:
                        nc.scalar.activation(rowQ[:, :], PS_B[0:1, 0:512], AF.Copy,
                                             scale=1.0 / D).then_inc(f1["ch"], 1)
                    else:
                        nc.scalar.copy(col4[:, :], col1[:, :]).then_inc(f1["ch"], 1)
                    nc.vector.wait_ge(f1["ch"], chp(t8, 2))
                    nc.vector.wait_ge(f1["rst"], 32 * NP1 * invb + 32 * t8)
                    if _P1CUT >= 3:
                        nc.vector.tensor_scalar(rowM[:, :], PS_C[0:1, 0:512], 1.0 / D, None, ALU.mult)
                        nc.vector.tensor_mul(rowT[:, :], rowM[:, :], rowM[:, :])
                        nc.vector.tensor_sub(rowT[:, :], rowQ[:, :], rowT[:, :])
                        nc.vector.tensor_scalar(rowT[:, :], rowT[:, :], LN_EPS, None,
                                                ALU.add).then_inc(f1["ch"], 1)
                    else:
                        nc.vector.engine_nop().then_inc(f1["ch"], 1)
                    nc.scalar.wait_ge(f1["ch"], chp(t8, 3))
                    if _P1CUT >= 4:
                        nc.scalar.activation(rowQ[:, :], rowT[:, :], AF.Sqrt).then_inc(f1["ch"], 1)
                    else:
                        nc.scalar.copy(col4[:, :], col1[:, :]).then_inc(f1["ch"], 1)
                    nc.vector.wait_ge(f1["ch"], chp(t8, 4))
                    if _P1CUT < 5:
                        nc.vector.engine_nop().then_inc(f1["ch"], 1)
                    if _P1CUT >= 5:
                        nc.vector.reciprocal(rowA[:, :], rowQ[:, :])
                    if _P1CUT >= 5:
                        nc.vector.tensor_mul(rowQ[:, :], rowA[:, :], rowA[:, :])
                        nc.vector.tensor_mul(rowQ[:, :], rowQ[:, :], rowT[:, :])
                        nc.vector.tensor_scalar(rowQ[:, :], rowQ[:, :], -0.5, 1.5, ALU.mult, ALU.add)
                        nc.vector.tensor_mul(rowA[:, :], rowA[:, :], rowQ[:, :])
                        nc.vector.tensor_mul(rowB[:, :], rowM[:, :], rowA[:, :])
                        nc.vector.tensor_scalar(rowB[:, :], rowB[:, :], -1.0, None,
                                                ALU.mult).then_inc(f1["ch"], 1)
                    nc.sync.wait_ge(f1["ch"], chp(t8, 5))
                    nc.sync.dma_start(bass.AP(arow_d, t8 * P1T, [[1, 1], [1, P1T]]),
                                      rowA[:, :]).then_inc(f1["rst"], 16)
                    nc.sync.dma_start(bass.AP(brow_d, t8 * P1T, [[1, 1], [1, P1T]]),
                                      rowB[:, :]).then_inc(f1["rst"], 16)
                    nc.tensor.wait_ge(f1["ch"], chp(t8, 5))
                    if _P1CUT >= 6:
                        nc.tensor.matmul(PS_A[:, 0:512], onesrS[:, :], rowA[:, :],
                                         start=True, stop=True)
                        nc.tensor.matmul(PS_A[:, 1024:1536], onesrS[:, :], rowB[:, :],
                                         start=True, stop=True)
                        nc.tensor.matmul(PS_A[0:8, 512:1024], seR[:, :], rowB[:, :],
                                         start=True, stop=True).then_inc(f1["ch"], 1)
                    else:
                        nc.tensor.nop().then_inc(f1["ch"], 1)
                    nc.scalar.wait_ge(f1["ch"], chp(t8, 6))
                    if _P1CUT >= 7:
                        nc.scalar.copy(p1bb[:, :], PS_A[:, 1024:1536])
                        nc.scalar.copy(sebS[:, :], PS_A[0:8, 512:1024])
                        nc.scalar.copy(rawS[:, :], PS_B[0:8, 512:1024]).then_inc(f1["ch"], 1)
                    else:
                        nc.scalar.copy(col4[:, :], col1[:, :]).then_inc(f1["ch"], 1)
                    nc.vector.wait_ge(f1["ch"], chp(t8, 7))
                    if _P1CUT >= 7:
                        nc.vector.tensor_copy(p1ab[:, :], PS_A[:, 0:512])
                        nc.vector.tensor_mul(lgbuf[:, :], rawS[:, :], p1ab[0:8, :])
                        nc.vector.tensor_add(lgbuf[:, :], lgbuf[:, :], sebS[:, :])
                        nc.vector.tensor_scalar(lgbuf[:, :], lgbuf[:, :], brC[:, :], None,
                                                ALU.add).then_inc(f1["ch"], 1)
                    else:
                        nc.vector.engine_nop().then_inc(f1["ch"], 1)
                    nc.vector.wait_ge(f1["yst"], 128 * invb + 16 * t8 + 16)
                    if _P1CUT >= 8:
                        for kc in range(8):
                            ins = nc.vector.tensor_mul(sqbuf[0][:, :],
                                                       p1x[par][:, kc * 512:(kc + 1) * 512],
                                                       p1ab[:, :])
                            if kc == 7:
                                ins.then_inc(f1["xf"], 1)
                            ins = nc.vector.tensor_add(ystg[par][:, kc * 512:(kc + 1) * 512],
                                                       sqbuf[0][:, :], p1bb[:, :])
                        ins.then_inc(f1["ch"], 1)
                    else:
                        nc.vector.engine_nop().then_inc(f1["xf"], 1)
                        nc.vector.engine_nop().then_inc(f1["ch"], 1)
                    nc.sync.wait_ge(f1["ch"], chp(t8, 9))
                    nc.sync.dma_start(
                        bass.AP(yt_d, t8 * P1T, [[TLOC, 128], [128 * TLOC, 8], [1, P1T]]),
                        ystg[par][:, :].rearrange("p (kc t) -> p kc t", kc=8)
                    ).then_inc(f1["yst"], 16)
                    for j in range(4):
                        jj = 4 * t8 + j
                        gj = invb * 32 + jj
                        parj = jj % 2
                        nc.tensor.wait_ge(f1["ch"], chp(t8, 8))
                        nc.tensor.wait_ge(f1["w1"], gj)
                        if _P1CUT >= 9:
                            nc.tensor.transpose(PS_A[0:128, 1536 + 8 * j:1544 + 8 * j],
                                                lgbuf[:, j * 128:(j + 1) * 128],
                                                identS[0:8, 0:8]).then_inc(f1["tt"], 1)
                        else:
                            nc.tensor.nop().then_inc(f1["tt"], 1)
                        NDV = 18
                        dvb = [NDV * gj]
                        dvk = [0]
                        def dvo(ins):
                            ins.then_inc(f1["dv"], 1)
                            dvk[0] += 1
                        def dvw():
                            nc.vector.wait_ge(f1["dv"], dvb[0] + dvk[0])
                        v = nc.vector
                        nc.vector.wait_ge(f1["tt"], gj + 1)
                        nc.vector.wait_ge(f1["ex"], gj)
                        dvo(v.tensor_copy(wtok[:, :], PS_A[:, 1536 + 8 * j:1544 + 8 * j])); dvw()
                        dvo(v.reduce_max(col1[:, :], wtok[:, :], AX)); dvw()
                        dvo(v.tensor_scalar(wtk2[:, :], wtok[:, :], col1[:, :], None, ALU.is_equal)); dvw()
                        dvo(v.reduce_sum(col2[:, :], wtk2[:, :], AX)); dvw()
                        dvo(v.tensor_scalar(wtkF[:, :], wtok[:, :], col1[:, :], None, ALU.is_lt)); dvw()
                        dvo(v.tensor_mul(wtk2[:, :], wtok[:, :], wtkF[:, :])); dvw()
                        dvo(v.tensor_scalar(wtkF[:, :], wtkF[:, :], 1e30, -1e30, ALU.mult, ALU.add)); dvw()
                        dvo(v.tensor_add(wtk2[:, :], wtk2[:, :], wtkF[:, :])); dvw()
                        dvo(v.reduce_max(col3[:, :], wtk2[:, :], AX)); dvw()
                        dvo(v.tensor_scalar(col4[:, :], col2[:, :], 1.5, None, ALU.is_gt)); dvw()
                        dvo(v.tensor_sub(col2[:, :], col1[:, :], col3[:, :])); dvw()
                        dvo(v.tensor_mul(col2[:, :], col2[:, :], col4[:, :])); dvw()
                        dvo(v.tensor_add(col2[:, :], col3[:, :], col2[:, :])); dvw()
                        dvo(v.tensor_scalar(wtkF[:, :], wtok[:, :], col2[:, :], None, ALU.is_ge)); dvw()
                        dvo(v.tensor_scalar(col3[:, :], col1[:, :], -1.0, None, ALU.mult))
                        nc.vector.engine_nop().then_inc(f1["w1"], 1)
                        nc.scalar.wait_ge(f1["dv"], NDV * gj + 15)
                        nc.scalar.wait_ge(f1["w1"], gj + 1)
                        if _P1CUT >= 11:
                            nc.scalar.activation(wtk2[:, :], wtok[:, :], AF.Exp,
                                                 bias=col3[:, :]).then_inc(f1["ex"], 1)
                        else:
                            nc.scalar.copy(col4[:, :], col1[:, :]).then_inc(f1["ex"], 1)
                        nc.vector.wait_ge(f1["ex"], gj + 1)
                        dvo(v.tensor_mul(wtok[:, :], wtk2[:, :], wtkF[:, :])); dvw()
                        dvo(v.reduce_sum(col3[:, :], wtok[:, :], AX)); dvw()
                        nc.vector.reciprocal(col3[:, :], col3[:, :]).then_inc(f1["dv"], 1)
                        dvk[0] += 1
                        dvw()
                        assert dvk[0] == NDV, dvk[0]
                        nc.vector.tensor_scalar(wtok[:, :], wtok[:, :], col3[:, :], 0.5,
                                                ALU.mult, ALU.mult).then_inc(f1["w2"], 1)
                        nc.tensor.wait_ge(f1["w2"], gj + 1)
                        nc.tensor.wait_ge(f1["wc"], gj + 2)
                        if _P1CUT >= 13:
                            nc.tensor.transpose(PS_C[0:8, 0:128], wtok[:, :], identS[:, :])
                            for e in range(8):
                                ins = nc.tensor.transpose(PS_B[0:1, 128 * e:128 * e + 128],
                                                          wtok[:, e:e + 1], identS[:, :])
                            ins.then_inc(f1["wtr"], 1)
                        else:
                            nc.tensor.nop().then_inc(f1["wtr"], 1)
                        nc.scalar.wait_ge(f1["wtr"], gj + 1)
                        nc.scalar.wait_ge(f1["wst"], 144 * gj + 144)
                        if _P1CUT >= 14:
                            nc.scalar.copy(w8rowstg[parj][:, :], PS_C[0:8, 0:128])
                            for e in range(8):
                                ins = nc.scalar.copy(wrowstg8[parj][0:1, 128 * e:128 * e + 128],
                                                     PS_B[0:1, 128 * e:128 * e + 128])
                            ins.then_inc(f1["wc"], 1)
                        else:
                            nc.scalar.copy(col4[:, :], col1[:, :]).then_inc(f1["wc"], 1)
                        nc.sync.wait_ge(f1["wc"], gj + 1 + 2)
                        for e in range(8):
                            nc.sync.dma_start(
                                bass.AP(wrow_d, e * TLOC + jj * 128, [[1, 1], [1, 128]]),
                                wrowstg8[parj][0:1, 128 * e:128 * e + 128]).then_inc(f1["wst"], 16)
                        nc.sync.dma_start(bass.AP(w8t_d, jj * 128, [[TLOC, 8], [1, 128]]),
                                          w8rowstg[parj][:, :]).then_inc(f1["wst"], 16)
        with nc.Block() as blk:
            @blk.sync
            def _(sp):
                sp.wait_ge(f1["wst"], 288 + 144 * 32 * (invb + 1))
                sp.wait_ge(f1["rst"], 32 * NP1 * (invb + 1))
                sp.wait_ge(f1["yst"], 32 + 16 * NP1 * (invb + 1))

    def emit_ffn_bias(invb):
        with nc.Block() as blk:
            @blk.gpsimd
            def _(g):
                g.dma_start(b2TS[:, :], b2[:, :]).then_inc(f2["b2l"], 16)
        with nc.Block():
            with nc.Fori(0, NFT // 2, engines=[ET.Pool, ET.PE, ET.Activation]) as ti:
                for par in range(2):
                    tt = ti * 2 + par
                    gt = invb * NFT + tt
                    toff = tt * FT
                    nc.gpsimd.wait_ge(f2["pb"], 8 * gt)
                    nc.gpsimd.dma_start(w8stg[par][:, :],
                                        bass.AP(w8t_d, toff, [[TLOC, 8], [1, FT]])).then_inc(f2["b8r"], 16)
                    nc.tensor.wait_ge(f2["b2l"], 16 * (invb + 1))
                    nc.tensor.wait_ge(f2["b8r"], 16 * gt + 16)
                    for dc in range(8):
                        bankd = PS_B[:, 0:256] if (dc % 2 == 0) else PS_B[:, 512:768]
                        nc.tensor.wait_ge(f2["pbe"], 8 * gt + dc + 1)
                        nc.tensor.matmul(bankd, b2TS[:, dc * 128:(dc + 1) * 128],
                                         w8stg[par][:, :], start=True, stop=True).then_inc(f2["pb"], 1)
                    nc.scalar.wait_ge(f2["bst"], 16 * gt + 16)
                    for dc in range(8):
                        bankd = PS_B[:, 0:256] if (dc % 2 == 0) else PS_B[:, 512:768]
                        nc.scalar.wait_ge(f2["pb"], 8 + 8 * gt + dc + 1)
                        nc.scalar.copy(xin[par][:, dc * 256:(dc + 1) * 256],
                                       bankd).then_inc(f2["pbe"], 1)
                    nc.gpsimd.wait_ge(f2["pbe"], 2 + 8 * gt + 8)
                    nc.gpsimd.dma_start(
                        bass.AP(xT_d, toff, [[TLOC, 128], [128 * TLOC, 8], [1, FT]]),
                        xin[par][:, :].rearrange("p (kc t) -> p kc t", kc=8),
                        accum_op=ALU.add).then_inc(f2["bst"], 16)
        with nc.Block() as blk:
            @blk.gpsimd
            def _(g):
                g.wait_ge(f2["bst"], 32 + 16 * NFT * (invb + 1))

    def emit_ffn_pass2(invb):
        NTI = NFT

        with nc.Block():
            with nc.Fori(0, NE, engines=[ET.Pool, ET.PE, ET.DVE, ET.Activation]) as e:
                ge = invb * NE + e
                nc.gpsimd.wait_ge(f2["wr"], 128 * ge)
                for half in range(2):
                    nc.gpsimd.dma_start(
                        w1S[:, half * 16384:(half + 1) * 16384].rearrange("p (kc m) -> p kc m", kc=4),
                        bass.AP(w1, e * D * HID + half * 4 * 128 * HID,
                                [[HID, 128], [128 * HID, 4], [1, HID]])).then_inc(f2["wld"], 16)
                for half in range(2):
                    nc.gpsimd.dma_start(
                        w2S[:, half * 16384:(half + 1) * 16384].rearrange("p (m d) -> p m d", m=16),
                        bass.AP(w2, e * HID * D + half * 16 * 128 * D,
                                [[D, 128], [128 * D, 16], [1, D]])).then_inc(f2["wld"], 16)
                nc.gpsimd.wait_ge(f2["hg"], 2 + 512 * ge)
                nc.gpsimd.tensor_copy(b1cur[:, :], bass.AP(b1S, e * 32, [[256, 128], [1, 32]]))
                with nc.Fori(0, NTI // 2, engines=[ET.Pool, ET.PE, ET.DVE, ET.Activation]) as ti:
                    for par in range(2):
                        tt = ti * 2 + par
                        gt = (ge * NTI) + tt
                        toff = tt * FT
                        nc.gpsimd.wait_ge(f2["bm"], gt + 1)
                        nc.gpsimd.dma_start(rowstg[par][0:1, 0:256],
                                            bass.AP(wrow_d, e * TLOC + toff, [[1, 1], [1, FT]])).then_inc(f2["rld"], 16)
                        nc.gpsimd.wait_ge(f2["yd"], gt + 1)
                        nc.gpsimd.dma_start(
                            yT[par][:, :].rearrange("p (kc t) -> p kc t", kc=8),
                            bass.AP(yt_d, toff, [[TLOC, 128], [128 * TLOC, 8], [1, FT]])
                        ).then_inc(f2["yld"], 16)
                        nc.tensor.wait_ge(f2["rld"], 16 * gt + 16)
                        nc.tensor.wait_ge(f2["abe"], gt + 1)
                        nc.tensor.matmul(PS_C[:, 0:256], onesrS[:, :],
                                         rowstg[par][0:1, 0:256], start=True, stop=True).then_inc(f2["bm"], 1)
                        nc.vector.wait_ge(f2["bm"], 2 + gt + 1)
                        nc.vector.tensor_copy(wb_sb[:, :], PS_C[:, 0:256]).then_inc(f2["abe"], 1)
                        nc.tensor.wait_ge(f2["yld"], 16 * gt + 16)
                        nc.tensor.wait_ge(f2["wld"], 64 * ge + 32)
                        for m in range(32):
                            bankh = PS_A[:, 0:256] if (m % 2 == 0) else PS_A[:, 512:768]
                            nc.tensor.wait_ge(f2["hg"], 32 * gt + m + 1)
                            for kc in range(8):
                                ins = nc.tensor.matmul(bankh, w1S[:, kc * 4096 + m * 128: kc * 4096 + (m + 1) * 128],
                                                       yT[par][:, kc * 256:(kc + 1) * 256],
                                                       start=(kc == 0), stop=(kc == 7))
                            ins.then_inc(f2["hr"], 1)
                        for m in range(32):
                            bankh = PS_A[:, 0:256] if (m % 2 == 0) else PS_A[:, 512:768]
                            nc.scalar.wait_ge(f2["hr"], 40 * gt + m + 1)
                            nc.scalar.activation(hT[:, m * 256:(m + 1) * 256], bankh, AF.Gelu,
                                                 bias=b1cur[:, m:m + 1]).then_inc(f2["hg"], 1)
                        nc.tensor.wait_ge(f2["wld"], 64 * ge + 64)
                        nc.tensor.wait_ge(f2["hg"], 2 + 32 * gt + 32)
                        for dc in range(8):
                            bankd = PS_B[:, 0:256] if (dc % 2 == 0) else PS_B[:, 512:768]
                            nc.tensor.wait_ge(f2["we"], 8 * gt + dc + 1)
                            for m in range(32):
                                ins = nc.tensor.matmul(bankd, w2S[:, m * 1024 + dc * 128: m * 1024 + (dc + 1) * 128],
                                                       hT[:, m * 256:(m + 1) * 256],
                                                       start=(m == 0), stop=(m == 31))
                                if dc == 0 and m == 0:
                                    ins.then_inc(f2["yd"], 1)
                            ins.then_inc(f2["wr"], 1)
                        nc.vector.wait_ge(f2["st"], 16 * gt + 16)
                        for dc in range(8):
                            bankd = PS_B[:, 0:256] if (dc % 2 == 0) else PS_B[:, 512:768]
                            stg = tmpbuf if (dc % 2 == 0) else tmp2
                            nc.scalar.wait_ge(f2["wr"], 8 * gt + dc + 1)
                            nc.scalar.wait_ge(f2["we"], 8 * gt + dc + 1)
                            nc.scalar.copy(stg[:, :], bankd).then_inc(f2["hr"], 1)
                            nc.vector.wait_ge(f2["hr"], 40 * gt + 32 + dc + 1)
                            nc.vector.tensor_mul(xin[par][:, dc * 256:(dc + 1) * 256],
                                                 stg[:, :], wb_sb[:, :]).then_inc(f2["we"], 1)
                        nc.gpsimd.wait_ge(f2["we"], 2 + 8 * gt + 8)
                        nc.gpsimd.dma_start(
                            bass.AP(xT_d, toff, [[TLOC, 128], [128 * TLOC, 8], [1, FT]]),
                            xin[par][:, :].rearrange("p (kc t) -> p kc t", kc=8),
                            accum_op=ALU.add).then_inc(f2["st"], 16)
        with nc.Block() as blk:
            @blk.gpsimd
            def _(g):
                g.wait_ge(f2["st"], 32 + 16 * NE * NTI * (invb + 1))

    av = {n: mk.sem("av_" + n) for n in
          ["ld", "ch", "gnc", "dv1", "dv2", "dv3", "axf", "wql", "qk", "qe",
           "vt", "vte", "s", "mx", "ex", "pb", "pt", "pte", "pv", "pe",
           "aod", "wol", "abm", "abe", "dm", "de", "dxl", "dst"]}
    attst = {"dv2": 0}

    with nc.Block() as blk:
        @blk.vector
        def _(v):
            v.sem_inc(av["dst"], 2 * 16)
            v.sem_inc(av["qk"], 2)
            v.sem_inc(av["qe"], 12)
            v.sem_inc(av["aod"], 16)
            v.sem_inc(av["de"], 2)
            v.sem_inc(av["dm"], 2)
            v.sem_inc(av["axf"], 2)
            v.sem_inc(av["gnc"], 2)

    def emit_attention(b):
        AT = 8
        boff = b * S
        CB = 8
        ch0 = b * CB * AT

        def chp(t, k):
            return ch0 + CB * t + k

        with nc.Block():
            with nc.Fori(0, AT // 2, engines=[ET.Pool, ET.PE, ET.DVE, ET.Activation]) as i4:
                for par in range(2):
                    t8 = i4 * 2 + par
                    g8 = b * AT + t8
                    nc.gpsimd.wait_ge(av["axf"], g8 + 1)
                    nc.gpsimd.dma_start(
                        xin[par][:, :].rearrange("p (kc t) -> p kc t", kc=8),
                        bass.AP(xT_d, boff + t8 * FT, [[TLOC, 128], [128 * TLOC, 8], [1, FT]])
                    ).then_inc(av["ld"], 16)
                    for kc in range(8):
                        nc.vector.wait_ge(av["ld"], 128 * b + (t8 + 1) * 16)
                        nc.vector.wait_ge(av["gnc"], 8 * g8 + kc + 1)
                        nc.vector.tensor_mul(tmp2[:, :] if kc % 2 == 0 else tmpbuf[:, :],
                                             xin[par][:, kc * 256:(kc + 1) * 256],
                                             xin[par][:, kc * 256:(kc + 1) * 256]).then_inc(av["dv1"], 1)
                    nc.tensor.wait_ge(av["ld"], 128 * b + (t8 + 1) * 16)
                    nc.tensor.wait_ge(av["ch"], chp(t8 - 1, 8))
                    for kc in range(8):
                        ins = nc.tensor.matmul(PS_C[0:1, 0:256], onescS[:, :],
                                               xin[par][:, kc * 256:(kc + 1) * 256],
                                               start=(kc == 0), stop=(kc == 7))
                    for kc in range(8):
                        nc.tensor.wait_ge(av["dv1"], 8 * g8 + kc + 1)
                        ins = nc.tensor.matmul(PS_B[0:1, 0:256], onescS[:, :],
                                               tmp2[:, :] if kc % 2 == 0 else tmpbuf[:, :],
                                               start=(kc == 0), stop=(kc == 7))
                        ins.then_inc(av["gnc"], 1)
                    nc.tensor.nop().then_inc(av["ch"], 1)
                    nc.scalar.wait_ge(av["ch"], chp(t8, 1))
                    nc.scalar.wait_ge(av["gnc"], 2 + 8 * g8 + 8)
                    nc.scalar.activation(rowQ[0:1, 0:256], PS_B[0:1, 0:256], AF.Copy,
                                         scale=1.0 / D).then_inc(av["ch"], 1)
                    nc.vector.wait_ge(av["ch"], chp(t8, 2))
                    nc.vector.tensor_scalar(rowM[0:1, 0:256], PS_C[0:1, 0:256], 1.0 / D, None, ALU.mult)
                    nc.vector.tensor_mul(rowT[0:1, 0:256], rowM[0:1, 0:256], rowM[0:1, 0:256])
                    nc.vector.tensor_sub(rowT[0:1, 0:256], rowQ[0:1, 0:256], rowT[0:1, 0:256])
                    nc.vector.tensor_scalar(rowT[0:1, 0:256], rowT[0:1, 0:256], LN_EPS, None,
                                            ALU.add).then_inc(av["ch"], 1)
                    nc.scalar.wait_ge(av["ch"], chp(t8, 3))
                    nc.scalar.activation(rowQ[0:1, 0:256], rowT[0:1, 0:256], AF.Sqrt).then_inc(av["ch"], 1)
                    a2s = bass.AP(a2R, t8 * 256, [[2048, 1], [1, 256]])
                    b2s = bass.AP(b2R, t8 * 256, [[2048, 1], [1, 256]])
                    nc.vector.wait_ge(av["ch"], chp(t8, 4))
                    nc.vector.reciprocal(a2s, rowQ[0:1, 0:256])
                    nc.vector.tensor_mul(rowQ[0:1, 0:256], a2s, a2s)
                    nc.vector.tensor_mul(rowQ[0:1, 0:256], rowQ[0:1, 0:256], rowT[0:1, 0:256])
                    nc.vector.tensor_scalar(rowQ[0:1, 0:256], rowQ[0:1, 0:256], -0.5, 1.5, ALU.mult, ALU.add)
                    nc.vector.tensor_mul(a2s, a2s, rowQ[0:1, 0:256])
                    nc.vector.tensor_mul(b2s, rowM[0:1, 0:256], a2s)
                    nc.vector.tensor_scalar(b2s, b2s, -1.0, None, ALU.mult).then_inc(av["ch"], 1)
                    nc.tensor.wait_ge(av["ch"], chp(t8, 5))
                    nc.tensor.matmul(PS_A[:, 1024:1280], onesrS[:, :], a2s,
                                     start=True, stop=True)
                    nc.tensor.matmul(PS_A[:, 1536:1792], onesrS[:, :], b2s,
                                     start=True, stop=True).then_inc(av["ch"], 1)
                    nc.scalar.wait_ge(av["ch"], chp(t8, 6))
                    nc.scalar.copy(bb_sb[:, :], PS_A[:, 1536:1792]).then_inc(av["ch"], 1)
                    nc.vector.wait_ge(av["ch"], chp(t8, 7))
                    nc.vector.tensor_copy(ab_sb[:, :], PS_A[:, 1024:1280])
                    for kc in range(8):
                        y2sl = bass.AP(y2S, kc * 2048 + t8 * 256, [[16384, 128], [1, 256]])
                        gps = bass.AP(gnpart, kc * 8 + t8, [[64, 128], [1, 1]])
                        gqs = bass.AP(gnqart, kc * 8 + t8, [[64, 128], [1, 1]])
                        nc.vector.tensor_mul(tmpbuf[:, :], xin[par][:, kc * 256:(kc + 1) * 256],
                                             ab_sb[:, :])
                        nc.vector.tensor_add(y2sl, tmpbuf[:, :], bb_sb[:, :])
                        nc.vector.reduce_sum(gps, y2sl, AX)
                        nc.vector.tensor_mul(tmp2[:, :], y2sl, y2sl)
                        ins = nc.vector.reduce_sum(gqs, tmp2[:, :], AX)
                    ins.then_inc(av["ch"], 1)
                    nc.vector.engine_nop().then_inc(av["axf"], 1)
        dv2 = [attst["dv2"]]

        def d2o(ins):
            ins.then_inc(av["dv2"], 1)
            dv2[0] += 1

        def d2w(eng):
            eng.wait_ge(av["dv2"], dv2[0])

        with nc.Block() as blk:
            @blk.vector
            def _(v):
                v.wait_ge(av["ch"], ch0 + CB * AT)
                for kc in range(8):
                    d2o(v.reduce_sum(gnS[:, kc:kc + 1], gnpart[:, kc * 8:(kc + 1) * 8], AX))
                    d2o(v.reduce_sum(gnQ[:, kc:kc + 1], gnqart[:, kc * 8:(kc + 1) * 8], AX))

            @blk.tensor
            def _(t):
                d2w(t)
                for kc in range(8):
                    t.matmul(PS_C[0:32, 0:1], gmatS[:, kc * 32:(kc + 1) * 32],
                             gnS[:, kc:kc + 1], start=(kc == 0), stop=(kc == 7))
                for kc in range(8):
                    ins = t.matmul(PS_C[0:32, 256:257], gmatS[:, kc * 32:(kc + 1) * 32],
                                   gnQ[:, kc:kc + 1], start=(kc == 0), stop=(kc == 7))
                ins.then_inc(av["dv2"], 1)
            dv2[0] += 1

            @blk.scalar
            def _(sc):
                d2w(sc)
                sc.activation(g32a[:, :], PS_C[0:32, 0:1], AF.Copy, scale=1.0 / 65536.0)
                d2o(sc.activation(g32b[:, :], PS_C[0:32, 256:257], AF.Copy,
                                  scale=1.0 / 65536.0))

            @blk.vector
            def _(v):
                d2w(v)
                d2o(v.tensor_mul(g32c[:, :], g32a[:, :], g32a[:, :])); d2w(v)
                d2o(v.tensor_sub(g32c[:, :], g32b[:, :], g32c[:, :])); d2w(v)
                d2o(v.tensor_scalar(g32c[:, :], g32c[:, :], GN_EPS, None, ALU.add))

            @blk.scalar
            def _(sc):
                d2w(sc)
                d2o(sc.activation(g32b[:, :], g32c[:, :], AF.Sqrt))

            @blk.vector
            def _(v):
                d2w(v)
                d2o(v.reciprocal(wcol1[:, :], g32b[:, :])); d2w(v)
                d2o(v.tensor_mul(g32b[:, :], wcol1[:, :], wcol1[:, :])); d2w(v)
                d2o(v.tensor_mul(g32b[:, :], g32b[:, :], g32c[:, :])); d2w(v)
                d2o(v.tensor_scalar(g32b[:, :], g32b[:, :], -0.5, 1.5, ALU.mult, ALU.add)); d2w(v)
                d2o(v.tensor_mul(g32b[:, :], wcol1[:, :], g32b[:, :]))

            @blk.tensor
            def _(t):
                d2w(t)
                for kc in range(8):
                    t.matmul(PS_C[0:128, kc:kc + 1], ematS[:, kc * 128:(kc + 1) * 128],
                             g32b[:, :], start=True, stop=True)
                for kc in range(8):
                    ins = t.matmul(PS_C[0:128, 16 + kc:17 + kc], ematS[:, kc * 128:(kc + 1) * 128],
                                   g32a[:, :], start=True, stop=True)
                ins.then_inc(av["dv2"], 1)
            dv2[0] += 1

            @blk.scalar
            def _(sc):
                d2w(sc)
                sc.copy(wtkF[:, :], PS_C[0:128, 0:8])
                d2o(sc.copy(wtk2[:, :], PS_C[0:128, 16:24]))

            @blk.vector
            def _(v):
                d2w(v)
                d2o(v.tensor_mul(aknS[:, :], wtkF[:, :], gamS[:, :])); d2w(v)
                d2o(v.tensor_mul(bknS[:, :], wtk2[:, :], aknS[:, :])); d2w(v)
                d2o(v.tensor_sub(bknS[:, :], betS[:, :], bknS[:, :])); d2w(v)
                for kc in range(8):
                    ins = v.tensor_scalar(y2S[:, kc * 2048:(kc + 1) * 2048],
                                          y2S[:, kc * 2048:(kc + 1) * 2048],
                                          aknS[:, kc:kc + 1], bknS[:, kc:kc + 1],
                                          ALU.mult, ALU.add)
                d2o(ins)
        attst["dv2"] = dv2[0]
        GN_DONE = dv2[0]

        with nc.Block():
            with nc.Fori(0, 4, engines=[ET.Pool, ET.PE, ET.DVE, ET.Activation, ET.SP]) as hp4:
                for par in range(2):
                    hp = hp4 * 2 + par
                    ghp = b * 8 + hp
                    nc.gpsimd.wait_ge(av["qe"], 12 * ghp)
                    for ci, cb in enumerate([0, D, 2 * D]):
                        nc.gpsimd.dma_start(
                            wqS[par][:, ci * 1024:(ci + 1) * 1024].rearrange("p (kc m) -> p kc m", kc=8),
                            bass.AP(wqkv, cb + hp * 128, [[3 * D, 128], [128 * 3 * D, 8], [1, 128]])
                        ).then_inc(av["wql"], 16)
                    for ci in range(3):
                        nc.gpsimd.dma_start(bqr[par][0:1, ci * 128:(ci + 1) * 128],
                                            bass.AP(bqkv, ci * D + hp * 128, [[1, 1], [1, 128]])
                                            ).then_inc(av["wql"], 16)
                    nc.tensor.wait_ge(av["wql"], 96 * ghp + 96)
                    for ci in range(3):
                        for t4 in range(4):
                            pq = PS_B[:, 0:512] if ((ci * 4 + t4) % 2 == 0) else PS_B[:, 512:1024]
                            nc.tensor.wait_ge(av["qe"], 12 + 12 * ghp + ci * 4 + t4 - 1)
                            for kc in range(8):
                                nc.tensor.matmul(
                                    pq, wqS[par][:, ci * 1024 + kc * 128:(ci) * 1024 + (kc + 1) * 128],
                                    y2S[:, kc * 2048 + t4 * 512: kc * 2048 + (t4 + 1) * 512],
                                    start=(kc == 0), stop=False)
                            ins = nc.tensor.matmul(pq, bqr[par][0:1, ci * 128:(ci + 1) * 128],
                                                   ones512B[:, :], start=False, stop=True)
                            ins.then_inc(av["qk"], 1)
                    for ci in range(3):
                        dstb = [q2S, k2S, v2S][ci]
                        scl = 0.125 if ci == 0 else 1.0
                        for t4 in range(4):
                            pq = PS_B[:, 0:512] if ((ci * 4 + t4) % 2 == 0) else PS_B[:, 512:1024]
                            nc.scalar.wait_ge(av["qk"], 2 + 12 * ghp + ci * 4 + t4 + 1)
                            nc.scalar.activation(dstb[:, t4 * 512:(t4 + 1) * 512], pq, AF.Copy,
                                                 scale=scl).then_inc(av["qe"], 1)
                    for wv in range(2):
                        nc.tensor.wait_ge(av["qe"], 12 + 12 * ghp + 12)
                        nc.tensor.wait_ge(av["vte"], 2 * ghp + wv)
                        for j in range(8):
                            jj = wv * 8 + j
                            ins = nc.tensor.transpose(PS_D[:, j * 128:(j + 1) * 128],
                                                      v2S[:, jj * 128:(jj + 1) * 128],
                                                      identBS[:, :])
                        ins.then_inc(av["vt"], 1)
                        nc.vector.wait_ge(av["vt"], 2 * ghp + wv + 1)
                        nc.vector.tensor_copy(vtS[:, wv * 1024:(wv + 1) * 1024],
                                              PS_D[:, :]).then_inc(av["vte"], 1)
                    for dl in range(2):
                        for qt in range(16):
                            uu = 32 * ghp + dl * 16 + qt
                            nc.tensor.wait_ge(av["ex"], uu)
                            if dl == 0 and qt == 0:
                                nc.tensor.wait_ge(av["dv2"], GN_DONE)
                            for kt in range(4):
                                ins = nc.tensor.matmul(
                                    PS_A[:, kt * 512:(kt + 1) * 512],
                                    q2S[dl * 64:dl * 64 + 64, qt * 128:(qt + 1) * 128],
                                    k2S[dl * 64:dl * 64 + 64, kt * 512:(kt + 1) * 512],
                                    start=True, stop=True)
                                ins.then_inc(av["s"], 1)
                            nc.vector.wait_ge(av["s"], 4 * uu + 4)
                            nc.vector.reduce_max(nmxcol[:, :], PS_A[:, 0:2048], AX,
                                                 negate=True).then_inc(av["mx"], 1)
                            nc.scalar.wait_ge(av["mx"], uu + 1)
                            nc.scalar.wait_ge(av["pb"], uu)
                            nc.scalar.activation(psbS[:, :], PS_A[:, 0:2048], AF.Exp,
                                                 bias=nmxcol[:, :],
                                                 accum_out=dncol[:, :]).then_inc(av["ex"], 1)
                            nc.vector.wait_ge(av["ex"], uu + 1)
                            nc.vector.reciprocal(rccol[:, :], dncol[:, :]).then_inc(av["dv3"], 1)
                            nc.vector.wait_ge(av["dv3"], uu + 1)
                            nc.vector.tensor_scalar(pbfS[:, :], psbS[:, :], rccol[:, :], None,
                                                    ALU.mult).then_inc(av["pb"], 1)
                            for wv in range(2):
                                nc.tensor.wait_ge(av["pb"], uu + 1)
                                nc.tensor.wait_ge(av["pte"], 2 * uu + wv)
                                if dl == 0 and qt == 0 and wv == 0:
                                    nc.tensor.wait_ge(av["vte"], 2 * ghp + 2)
                                for j in range(8):
                                    jj = wv * 8 + j
                                    ins = nc.tensor.transpose(PS_D[:, j * 128:(j + 1) * 128],
                                                              pbfS[:, jj * 128:(jj + 1) * 128],
                                                              identBS[:, :])
                                ins.then_inc(av["pt"], 1)
                                nc.vector.wait_ge(av["pt"], 2 * uu + wv + 1)
                                nc.vector.tensor_copy(ptS[:, wv * 1024:(wv + 1) * 1024],
                                                      PS_D[:, :]).then_inc(av["pte"], 1)
                            nc.tensor.wait_ge(av["pte"], 2 * uu + 2)
                            nc.tensor.wait_ge(av["pe"], uu)
                            for kcc in range(16):
                                ins = nc.tensor.matmul(
                                    PS_C[dl * 64:dl * 64 + 64, (qt % 2) * 128:(qt % 2) * 128 + 128],
                                    vtS[:, kcc * 128 + dl * 64: kcc * 128 + dl * 64 + 64],
                                    ptS[:, kcc * 128:(kcc + 1) * 128],
                                    start=(kcc == 0), stop=(kcc == 15))
                            ins.then_inc(av["pv"], 1)
                            nc.scalar.wait_ge(av["pv"], uu + 1)
                            if dl == 0 and qt == 0:
                                nc.scalar.wait_ge(av["aod"], 16 * ghp)
                            nc.scalar.copy(aostg[dl * 64:dl * 64 + 64, qt * 128:(qt + 1) * 128],
                                           PS_C[dl * 64:dl * 64 + 64, (qt % 2) * 128:(qt % 2) * 128 + 128]
                                           ).then_inc(av["pe"], 1)
                    nc.sync.wait_ge(av["pe"], 32 * ghp + 32)
                    nc.sync.dma_start(
                        bass.AP(aoS, hp * 2048, [[16384, 128], [1, 2048]]),
                        aostg[:, :]).then_inc(av["aod"], 16)
        with nc.Block() as blk:
            if b == 0:
                @blk.gpsimd
                def _(g):
                    g.dma_start(
                        woS[:, :].rearrange("p (kc m) -> p kc m", kc=8),
                        bass.AP(wo, 0, [[D, 128], [128 * D, 8], [1, D]])).then_inc(av["wol"], 16)
            @blk.sync
            def _(sp):
                sp.wait_ge(av["aod"], 16 + 128 * (b + 1))
        with nc.Block():
            with nc.Fori(0, 4, engines=[ET.Pool, ET.PE, ET.DVE, ET.Activation, ET.SP]) as ti:
                for par in range(2):
                    tt = ti * 2 + par
                    gt = b * 8 + tt
                    toff = boff + tt * FT
                    nc.gpsimd.wait_ge(av["dst"], 16 * gt + 16)
                    nc.gpsimd.dma_start(
                        xin[par][:, :].rearrange("p (kc t) -> p kc t", kc=8),
                        bass.AP(xT_d, toff, [[TLOC, 128], [128 * TLOC, 8], [1, FT]])
                    ).then_inc(av["dxl"], 16)
                    nc.tensor.wait_ge(av["wol"], 16)
                    nc.tensor.wait_ge(av["abe"], 2 * gt)
                    nc.tensor.matmul(PS_A[:, 1024:1280], onesrS[:, :],
                                     bass.AP(a2R, tt * 256, [[2048, 1], [1, 256]]),
                                     start=True, stop=True)
                    nc.tensor.matmul(PS_A[:, 1536:1792], onesrS[:, :],
                                     bass.AP(b2R, tt * 256, [[2048, 1], [1, 256]]),
                                     start=True, stop=True).then_inc(av["abm"], 1)
                    nc.scalar.wait_ge(av["abm"], gt + 1)
                    nc.scalar.wait_ge(av["de"], 8 * gt + 2)
                    nc.scalar.copy(bb_sb[:, :], PS_A[:, 1536:1792]).then_inc(av["abe"], 1)
                    nc.vector.wait_ge(av["abm"], gt + 1)
                    nc.vector.tensor_copy(ab_sb[:, :], PS_A[:, 1024:1280]).then_inc(av["abe"], 1)
                    nc.vector.wait_ge(av["dxl"], 16 * gt + 16)
                    for dc in range(8):
                        pd = PS_B[:, 0:256] if (dc % 2 == 0) else PS_B[:, 512:768]
                        nc.tensor.wait_ge(av["de"], 8 * gt + dc + 1)
                        for kc in range(8):
                            nc.tensor.matmul(
                                pd, woS[:, kc * 1024 + dc * 128: kc * 1024 + (dc + 1) * 128],
                                bass.AP(aoS, kc * 2048 + tt * 256, [[16384, 128], [1, 256]]),
                                start=(kc == 0), stop=False)
                        ins = nc.tensor.matmul(pd, borowB[0:1, dc * 128:(dc + 1) * 128],
                                               ones512B[0:1, 0:256], start=False, stop=True)
                        ins.then_inc(av["dm"], 1)
                        nc.scalar.wait_ge(av["dm"], 2 + 16 * gt + 2 * dc + 1)
                        nc.scalar.wait_ge(av["de"], 8 * gt + dc + 1)
                        nc.scalar.copy(dsb[dc % 2][:, :], pd).then_inc(av["dm"], 1)
                        nc.vector.wait_ge(av["dm"], 2 + 16 * gt + 2 * dc + 2)
                        xsl = xin[par][:, dc * 256:(dc + 1) * 256]
                        nc.vector.tensor_mul(tmpbuf[:, :], xsl, ab_sb[:, :])
                        nc.vector.tensor_add(tmpbuf[:, :], tmpbuf[:, :], bb_sb[:, :])
                        nc.vector.tensor_add(xsl, xsl, tmpbuf[:, :])
                        nc.vector.tensor_add(xsl, xsl, dsb[dc % 2][:, :]).then_inc(av["de"], 1)
                    nc.sync.wait_ge(av["de"], 2 + 8 * gt + 8)
                    nc.sync.dma_start(
                        bass.AP(xT_d, toff, [[TLOC, 128], [128 * TLOC, 8], [1, FT]]),
                        xin[par][:, :].rearrange("p (kc t) -> p kc t", kc=8)
                    ).then_inc(av["dst"], 16)
        with nc.Block() as blk:
            @blk.sync
            def _(sp):
                sp.wait_ge(av["dst"], 32 + 16 * 8 * (b + 1))

    if stage >= 2:
        emit_ffn_pass1(0)
    if stage >= 3:
        emit_ffn_bias(0)
    if stage >= 4:
        emit_ffn_pass2(0)
    if stage >= 5:
        emit_attention(0)
        emit_attention(1)
    if stage >= 6:
        emit_ffn_pass1(1)
        emit_ffn_bias(1)
        emit_ffn_pass2(1)

    if stage < 99 and dbg_d is not None and stage >= 2:
        dbg2sem = mk.sem("dbg2sem")
        with nc.Block() as blk:
            @blk.sync
            def _(sp):
                sp.dma_start(bass.AP(dbg_d, 0, [[TLOC, 8], [1, TLOC]]),
                             wrow_d[:, :]).then_inc(dbg2sem, 16)
                sp.dma_start(bass.AP(dbg_d, 8 * TLOC, [[TLOC, 8], [1, TLOC]]),
                             w8t_d[:, :]).then_inc(dbg2sem, 16)
                sp.dma_start(bass.AP(dbg_d, 16 * TLOC, [[TLOC, 1], [1, TLOC]]),
                             bass.AP(arow_d, 0, [[TLOC, 1], [1, TLOC]])).then_inc(dbg2sem, 16)
                sp.dma_start(bass.AP(dbg_d, 17 * TLOC, [[TLOC, 1], [1, TLOC]]),
                             bass.AP(brow_d, 0, [[TLOC, 1], [1, TLOC]])).then_inc(dbg2sem, 16)
                sp.dma_start(bass.AP(dbg_d, 32 * TLOC, [[TLOC, 992], [1, TLOC]]),
                             bass.AP(xT_d, 0, [[TLOC, 992], [1, TLOC]])).then_inc(dbg2sem, 16)
                sp.dma_start(dbg2_d[:, :], yt_d[:, :]).then_inc(dbg2sem, 16)
                sp.wait_ge(dbg2sem, 96)

    p6_ld = mk.sem("p6_ld")
    p6_tp = mk.sem("p6_tp")
    p6_ev = mk.sem("p6_ev")
    p6_st = mk.sem("p6_st")
    with nc.Block() as blk:
        @blk.sync
        def _(sp):
            sp.sem_inc(p6_st, 2 * 16)
        @blk.vector
        def _(v):
            v.sem_inc(p6_ev, 2)
        @blk.tensor
        def _(t):
            t.sem_inc(p6_tp, 2)
    with nc.Block():
        with nc.Fori(0, 16, engines=[ET.Pool, ET.PE, ET.DVE, ET.SP]) as i:
            for par in range(2):
                tix = i * 2 + par
                nc.gpsimd.wait_ge(p6_tp, tix + 1)
                nc.gpsimd.dma_start(
                    tin[par][:, :].rearrange("p (kc t) -> p kc t", kc=8),
                    bass.AP(xT_d, tix * 128, [[TLOC, 128], [128 * TLOC, 8], [1, 128]]),
                ).then_inc(p6_ld, 16)
            for par in range(2):
                tix = i * 2 + par
                psq = PS_B[:, :] if par == 0 else PS_A[:, 0:1024]
                nc.tensor.wait_ge(p6_ld, (tix + 1) * 16)
                nc.tensor.wait_ge(p6_ev, tix + 1)
                for j in range(8):
                    ins = nc.tensor.transpose(psq[:, j * 128:(j + 1) * 128],
                                              tin[par][:, j * 128:(j + 1) * 128],
                                              identS[:, :])
                ins.then_inc(p6_tp, 1)
            for par in range(2):
                tix = i * 2 + par
                psq = PS_B[:, :] if par == 0 else PS_A[:, 0:1024]
                nc.vector.wait_ge(p6_tp, tix + 3)
                nc.vector.wait_ge(p6_st, 16 * tix + 16)
                nc.vector.tensor_copy(tout[par][:, :], psq).then_inc(p6_ev, 1)
            for par in range(2):
                tix = i * 2 + par
                nc.sync.wait_ge(p6_ev, tix + 3)
                nc.sync.dma_start(bass.AP(out, tix * 128 * D, [[D, 128], [1, D]]),
                                  tout[par][:, :]).then_inc(p6_st, 16)
    with nc.Block() as blk:
        @blk.sync
        def _(sp):
            sp.wait_ge(p6_st, 2 * 16 + 32 * 16)

    nc.finalize()
    return mk


_CACHE = {}


def _get_built():
    key = _STAGE
    if key not in _CACHE:
        _CACHE[key] = build(_STAGE)
    return _CACHE[key]


def kernel(**inputs):
    mk = _get_built()
    x = np.asarray(inputs["L0_attn_x"], dtype=np.float32)
    base = {
        "wqkv": np.ascontiguousarray(inputs["L0_attn_W_qkv"], np.float32),
        "bqkv": np.ascontiguousarray(inputs["L0_attn_b_qkv"], np.float32),
        "wo": np.ascontiguousarray(inputs["L0_attn_W_o"], np.float32),
        "bo": np.ascontiguousarray(inputs["L0_attn_b_o"], np.float32),
        "gng": np.ascontiguousarray(inputs["L0_attn_gn_gamma"], np.float32),
        "gnb": np.ascontiguousarray(inputs["L0_attn_gn_beta"], np.float32),
        "w1": np.ascontiguousarray(inputs["L0_ffn_W1"], np.float32),
        "b1": np.ascontiguousarray(inputs["L0_ffn_b1"], np.float32),
        "w2": np.ascontiguousarray(inputs["L0_ffn_W2"], np.float32),
        "b2": np.ascontiguousarray(inputs["L0_ffn_b2"], np.float32),
        "wr": np.ascontiguousarray(inputs["L0_ffn_Wr"], np.float32),
        "br": np.ascontiguousarray(inputs["L0_ffn_br"], np.float32),
    }
    in_maps = []
    for c in range(NCORES):
        m = dict(base)
        m["x"] = np.ascontiguousarray(x[NB * c:NB * (c + 1)].reshape(TLOC, D))
        in_maps.append(m)
    res = run_bass_kernel_spmd(mk.nc, in_maps, core_ids=list(range(NCORES)))
    outs = [r["out"].reshape(NB, S, D) for r in res.results]
    return np.concatenate(outs, axis=0)
